# version 16
# baseline (speedup 1.0000x reference)
import re
import numpy as np
import ml_dtypes
import concourse.bass as bass
import concourse.mybir as mybir
from concourse.bass_utils import run_bass_kernel_spmd

F32 = mybir.dt.float32
BF16 = mybir.dt.bfloat16
AF = mybir.ActivationFunctionType
ALU = mybir.AluOpType

D, DI, S, T = 1024, 2048, 16, 4096
NJ, NK = DI // 128, D // 128
TC, NC = 512, 8
TCB, NCB = 2048, 2
EPS = 1e-5
ENGS = ("sync", "vector", "scalar", "tensor", "gpsimd")
ESEM = {"sync": "SY", "vector": "SV", "scalar": "SS", "tensor": "ST",
        "gpsimd": "SP"}


def _merge(dst, src):
    for k, v in src.items():
        if dst.get(k, -1) < v:
            dst[k] = v


class Em:
    def __init__(self):
        self.ops = {e: [] for e in ENGS}
        self.cnt = {}
        self.know = {e: {} for e in ENGS}
        self.snap = {}
        self.buf = {}
        self.pending_fence = None
        self.fenced = set()
        self.streams = set()

    def fence(self):
        self.pending_fence = dict(self.cnt)
        self.fenced = set()

    def emit(self, eng, fn, reads=(), writes=(), dma=False, stream=None):
        need = {}

        def add(sv):
            if need.get(sv[0], -1) < sv[1]:
                need[sv[0]] = sv[1]

        for k in reads:
            b = self.buf.get(k)
            if b and b["w"]:
                add(b["w"])
        for k in writes:
            b = self.buf.get(k)
            if b:
                if b["w"]:
                    add(b["w"])
                for sv in b["r"].items():
                    add(sv)
        if self.pending_fence is not None and eng not in self.fenced:
            for s, v in self.pending_fence.items():
                if v > 0:
                    add((s, v))
            self.fenced.add(eng)
            if len(self.fenced) == len(ENGS):
                self.pending_fence = None
        if dma:
            assert writes, "dma needs a write key to derive its stream"
            sem = "D_" + (stream or re.sub(r"[0-9.].*$", "", writes[0]))
            self.streams.add(sem)
            amt = 16
            prev = self.cnt.get(sem, 0)
            if prev > 0:
                add((sem, prev))
        else:
            sem = ESEM[eng]
            amt = 1
        know = self.know[eng]
        waits = [(s, v) for s, v in need.items() if know.get(s, -1) < v]
        for s, v in waits:
            _merge(know, self.snap.get((s, v), {}))
            if know.get(s, -1) < v:
                know[s] = v
        self.cnt[sem] = self.cnt.get(sem, 0) + amt
        my = self.cnt[sem]
        snap = dict(know)
        _merge(snap, self.snap.get((sem, my - amt), {}))
        snap[sem] = my
        self.snap[(sem, my)] = snap
        self.ops[eng].append((waits, fn, sem, amt))
        for k in writes:
            self.buf[k] = {"w": (sem, my), "r": {}}
        for k in reads:
            b = self.buf.setdefault(k, {"w": None, "r": {}})
            if b["r"].get(sem, -1) < my:
                b["r"][sem] = my
        return (sem, my)

    def dma(self, eng, out_ap, in_ap, reads=(), writes=(), stream=None):
        return self.emit(eng, lambda h: h.dma_start(out=out_ap, in_=in_ap),
                         reads=reads, writes=writes, dma=True, stream=stream)

    def run(self, eng, handle, smap):
        for waits, fn, sem, amt in self.ops[eng]:
            for s, v in waits:
                handle.wait_ge(smap[s], v)
            ins = fn(handle)
            if ins is not None:
                ins.then_inc(smap[sem], amt)


class Reg:

    def __init__(self, t, nbytes):
        self.t, self.cap, self.off = t, nbytes, 0

    def reset(self):
        self.off = 0

    def a(self, shape, dt):
        esz = 2 if dt == BF16 else 4
        n = int(np.prod(shape[1:])) * esz
        o = (self.off + 63) & ~63
        self.off = o + n
        assert self.off <= self.cap, f"region overflow {self.off}>{self.cap}"
        ap = self.t[:, o // 2:(o + n) // 2]
        if dt != BF16:
            ap = ap.bitcast(dt)
        if len(shape) > 2:
            free = shape[1:]
            letters = "abcdefg"[:len(free)]
            spec = "p (" + " ".join(letters) + ") -> p " + " ".join(letters)
            ap = ap.rearrange(spec, **{letters[i]: free[i]
                                       for i in range(len(free))})
        return ap


def build_program():
    nc = bass.Bass("TRN2", target_bir_lowering=False)

    def din(name, shape, dt=F32):
        return nc.dram_tensor(name, list(shape), dt, kind="ExternalInput").ap()

    x_tm = din("x_tm", [T, D])
    w_in = din("w_in", [D, 2 * DI])
    w_dt = din("w_dt", [DI, DI])
    w_out = din("w_out", [DI, D])
    w_f1 = din("w_f1", [D, D])
    w_f2 = din("w_f2", [D, D])
    w_f3 = din("w_f3", [D, D])
    xpw_t = din("xpw_t", [128, NJ * S])
    cw_t = din("cw_t", [128, 4 * NJ])
    ddg_t = din("ddg_t", [128, NJ * 128])
    alog_t = din("alog_t", [128, NJ * S])
    prm_t = din("prm_t", [128, 64])
    idb_t = din("idb_t", [128, 128], BF16)
    idf_t = din("idf_t", [128, 128], F32)
    out_tm = nc.dram_tensor("out_tm", [T, D], F32, kind="ExternalOutput").ap()

    def scr(name, shape, dt):
        return nc.dram_tensor(name, list(shape), dt, kind="Internal").ap()

    wf_dt = scr("wf_dt", [DI, DI], BF16)
    wf_out = scr("wf_out", [DI, D], BF16)
    wf_f = [scr(f"wf_f{i}", [D, D], BF16) for i in range(3)]
    xn_sp = scr("xn_sp", [128, NK, T], BF16)
    u_sp = scr("u_sp", [128, NJ, T], BF16)
    up_sp = scr("up_sp", [128, NJ, T], BF16)
    bmrep_sp = scr("bmrep_sp", [128, S, T], BF16)
    sz_sp = scr("sz_sp", [128, NJ, T], BF16)
    d_sp = scr("d_sp", [128, NJ, T], F32)
    w_sp = scr("w_sp", [128, NJ, T], BF16)
    yb_sp = scr("yb_sp", [128, NJ, T], BF16)

    em = Em()

    from contextlib import ExitStack
    with ExitStack() as stack:
        e = stack.enter_context
        wrega_t = e(nc.sbuf_tensor("wrega", [128, 32768], BF16))
        wregb_t = e(nc.sbuf_tensor("wregb", [128, 12288], BF16))
        work_t = e(nc.sbuf_tensor("work", [128, 43008], BF16))
        prm = e(nc.sbuf_tensor("prm", [128, 64], F32))
        cwt_s = e(nc.sbuf_tensor("cwt", [128, 64], F32))
        cells = e(nc.sbuf_tensor("cells", [128, 32], F32))
        aneg = e(nc.sbuf_tensor("aneg", [128, NJ * S], F32))
        ddg = e(nc.sbuf_tensor("ddg", [128, NJ * 128], BF16))
        xp16_s = e(nc.sbuf_tensor("xp16", [128, NJ * S], BF16))
        idb = e(nc.sbuf_tensor("idb", [128, 128], BF16))
        idf = e(nc.sbuf_tensor("idf", [128, 128], F32))
        bsb = e(nc.sbuf_tensor("bsb", [16, T], BF16))
        hstt = e(nc.sbuf_tensor("hstt", [128, NJ * S], F32))
        psf = e(nc.psum_tensor("psf", [128, 3584], F32))
        psb = e(nc.psum_tensor("psb", [128, 1024], BF16))
        blk = e(nc.Block())
        WA = Reg(wrega_t, 65536)
        WB = Reg(wregb_t, 24576)
        WK = Reg(work_t, 86016)
        aneg3 = aneg.rearrange("p (j s) -> p j s", j=NJ)
        hst3 = hstt.rearrange("p (j s) -> p j s", j=NJ)
        ddg3 = ddg.rearrange("p (j c) -> p j c", j=NJ)
        xp3 = xp16_s.rearrange("p (j s) -> p j s", j=NJ)
        cwt3 = cwt_s.rearrange("p (t j) -> p t j", t=4)
        P_CB, P_DTB, P_GAM, P_BET, P_FB = 0, 16, 32, 40, 48

        def prmv(i):
            return prm[:, i:i + 1]

        def cl(i):
            return cells[:, i:i + 1]

        def bankf(i, w=512):
            return psf[:, i * 512:i * 512 + w]

        wstg = [WB.a([128, 2048], F32) for _ in range(2)]
        wcv = [WB.a([128, 2048], BF16) for _ in range(2)]
        win_sb = WA.a([128, 8, 2 * DI], BF16)

        C_EPS, C_INVD, C_NEGH, C_ONE = 16, 17, 18, 19
        em.emit("vector", lambda h: h.memset(cl(C_EPS), EPS),
                writes=[f"c{C_EPS}"])
        em.emit("vector", lambda h: h.memset(cl(C_INVD), 1.0 / D),
                writes=[f"c{C_INVD}"])
        em.emit("vector", lambda h: h.memset(cl(C_NEGH), -0.5),
                writes=[f"c{C_NEGH}"])
        em.emit("vector", lambda h: h.memset(cl(C_ONE), 1.0),
                writes=[f"c{C_ONE}"])

        em.dma("sync", prm[:], prm_t, writes=["prm"])
        em.dma("sync", cwt_s[:], cw_t, writes=["cwt"])
        em.dma("sync", idb[:], idb_t, writes=["idb"])
        em.dma("sync", idf[:], idf_t, writes=["idf"])
        em.dma("sync", aneg[:], alog_t, writes=["aneg"])
        em.emit("scalar", lambda h: h.activation(aneg[:], aneg[:], AF.Exp),
                reads=["aneg"], writes=["aneg"])
        em.emit("vector", lambda h: h.tensor_scalar(
            aneg[:], aneg[:], -1.0, None, ALU.mult),
            reads=["aneg"], writes=["aneg"])
        em.dma("sync", wstg[0][:, 0:NJ * 128], ddg_t, writes=["wstg0"])
        em.emit("scalar", lambda h: h.activation(
            ddg[:], wstg[0][:, 0:NJ * 128], AF.Copy),
            reads=["wstg0"], writes=["ddg"])
        em.dma("sync", wstg[0][:, 0:NJ * S], xpw_t, writes=["wstg0"])
        em.emit("scalar", lambda h: h.activation(
            xp16_s[:], wstg[0][:, 0:NJ * S], AF.Copy),
            reads=["wstg0"], writes=["xp16"])

        w0_jobs = []
        for k in range(NK):
            for hh in range(2):
                w0_jobs.append(("win", k, hh, None))
        for k in range(NJ):
            w0_jobs.append(("dt", k, 0, wf_dt))
        for k in range(NJ):
            w0_jobs.append(("out", k, 0, wf_out))
        for i in range(3):
            for k in range(NK):
                w0_jobs.append((f"f{i}", k, 0, wf_f[i]))
        w0_state = {"i": 0}

        def drain_w0(njobs):
            for _ in range(njobs):
                i = w0_state["i"]
                if i >= len(w0_jobs):
                    return
                w0_state["i"] = i + 1
                kind, k, hh, dst = w0_jobs[i]
                par = i % 2
                if kind == "win":
                    src = w_in[k * 128:(k + 1) * 128, hh * 2048:(hh + 1) * 2048]
                    em.dma("sync", wstg[par][:], src,
                           reads=[], writes=[f"wstg{par}"],
                           stream=f"wstg{par}")
                    em.emit("scalar", lambda h, k=k, hh=hh, par=par:
                            h.activation(
                                win_sb[:, k, hh * 2048:(hh + 1) * 2048],
                                wstg[par][:], AF.Copy),
                            reads=[f"wstg{par}"], writes=["win"])
                else:
                    if kind == "dt":
                        src = w_dt[k * 128:(k + 1) * 128, :]
                        ncol = 2048
                        dkey = "wfdt"
                    elif kind == "out":
                        src = w_out[k * 128:(k + 1) * 128, :]
                        ncol = 1024
                        dkey = "wfout"
                    else:
                        src = (w_f1, w_f2, w_f3)[int(kind[1])][
                            k * 128:(k + 1) * 128, :]
                        ncol = 1024
                        dkey = "wf" + kind
                    em.dma("sync", wstg[par][:, 0:ncol], src,
                           writes=[f"wstg{par}"], stream=f"wstg{par}")
                    em.emit("scalar", lambda h, p=par, n=ncol: h.activation(
                        wcv[p][:, 0:n], wstg[p][:, 0:n], AF.Copy),
                        reads=[f"wstg{par}"], writes=[f"wcv{par}"])
                    em.dma("scalar", dst[k * 128:(k + 1) * 128, :],
                           wcv[par][:, 0:ncol],
                           reads=[f"wcv{par}"], writes=[dkey],
                           stream=f"wsc{par}")

        drain_w0(16)

        WK.reset()
        xb = [WK.a([128, D], F32) for _ in range(2)]
        dump = WK.a([128, D], F32)
        xn_tm = [WK.a([128, D], BF16) for _ in range(2)]
        xh_tm = [WK.a([128, D], BF16) for _ in range(2)]
        xh_fm = [WK.a([128, NK, TC], BF16) for _ in range(2)]
        xn_fm = WK.a([128, NK, TC], BF16)
        upre = [WK.a([128, NJ, TC], BF16) for _ in range(2)]
        szb = [WK.a([128, TC], BF16) for _ in range(2)]

        CSM, CSS, CMU, CVAR, CR, CNMR, CT6 = 0, 1, 2, 3, 4, 5, 6

        def ln_pass(src_ap, src_key, dst_ap, dst_key, base):
            sm, ss_, mu, var, r, nmr, t6 = (base + i for i in range(7))
            em.emit("vector", lambda h: h.tensor_tensor(
                dump[:], src_ap, src_ap, ALU.mult),
                reads=[src_key], writes=["dump"])
            em.emit("vector", lambda h: h.reduce_sum(
                cl(ss_), dump[:], mybir.AxisListType.X),
                reads=["dump"], writes=[f"c{ss_}"])
            em.emit("vector", lambda h: h.reduce_sum(
                cl(sm), src_ap, mybir.AxisListType.X),
                reads=[src_key], writes=[f"c{sm}"])
            em.emit("vector", lambda h: h.tensor_scalar(
                cl(mu), cl(sm), 1.0 / D, None, ALU.mult),
                reads=[f"c{sm}"], writes=[f"c{mu}"])
            em.emit("scalar", lambda h: h.activation(
                cl(t6), cl(mu), AF.Square),
                reads=[f"c{mu}"], writes=[f"c{t6}"])
            em.emit("vector", lambda h: h.scalar_tensor_tensor(
                cl(var), cl(ss_), 1.0 / D, cl(t6), ALU.mult, ALU.subtract),
                reads=[f"c{ss_}", f"c{t6}"], writes=[f"c{var}"])
            em.emit("scalar", lambda h: h.activation(
                cl(t6), cl(var), AF.Ln, bias=cl(C_EPS)),
                reads=[f"c{var}", f"c{C_EPS}"], writes=[f"c{t6}"])
            em.emit("scalar", lambda h: h.activation(
                cl(r), cl(t6), AF.Exp, scale=cl(C_NEGH)),
                reads=[f"c{t6}", f"c{C_NEGH}"], writes=[f"c{r}"])
            em.emit("vector", lambda h: h.scalar_tensor_tensor(
                cl(nmr), cl(mu), -1.0, cl(r), ALU.mult, ALU.mult),
                reads=[f"c{mu}", f"c{r}"], writes=[f"c{nmr}"])
            em.emit("vector", lambda h: h.tensor_scalar(
                dst_ap, src_ap, cl(r), cl(nmr), ALU.mult, ALU.add),
                reads=[src_key, f"c{r}", f"c{nmr}"], writes=[dst_key])

        for c in range(NC):
            t0 = c * TC
            for ti in range(4):
                par = ti % 2
                base = par * 8
                rows = slice(t0 + ti * 128, t0 + (ti + 1) * 128)
                em.dma("sync", xb[par][:], x_tm[rows, :],
                       writes=[f"xb{par}"], stream=f"xb{par}")
                ln_pass(xb[par][:], f"xb{par}", xn_tm[par][:], f"xn{par}",
                        base)
                ln_pass(xn_tm[par][:], f"xn{par}", xh_tm[par][:],
                        f"xh{par}", base)
                def trfn(h, p=par):
                    ins = None
                    for dj in range(NK):
                        ins = h.transpose(
                            psb[:, dj * 128:(dj + 1) * 128],
                            xh_tm[p][:, dj * 128:(dj + 1) * 128], idb[:])
                    return ins
                em.emit("tensor", trfn, reads=[f"xh{par}", "idb"],
                        writes=["psbT"])
                cp = c % 2
                for k in range(NK):
                    em.emit("vector", lambda h, k=k, ti=ti, cp=cp:
                            h.tensor_scalar(
                                xh_fm[cp][:, k, ti * 128:(ti + 1) * 128],
                                psb[:, k * 128:(k + 1) * 128],
                                prmv(P_GAM + k), prmv(P_BET + k),
                                ALU.mult, ALU.add),
                            reads=["psbT", "prm"],
                            writes=[f"xhf{cp}.{ti}.{k}"])
                def trfn2(h, p=par):
                    ins = None
                    for dj in range(NK):
                        ins = h.transpose(
                            psb[:, dj * 128:(dj + 1) * 128],
                            xn_tm[p][:, dj * 128:(dj + 1) * 128], idb[:])
                    return ins
                em.emit("tensor", trfn2, reads=[f"xn{par}", "idb"],
                        writes=["psbT"])
                em.emit("scalar", lambda h, ti=ti: h.activation(
                    xn_fm[:, :, ti * 128:(ti + 1) * 128],
                    psb.ap().rearrange("p (k t) -> p k t", k=NK),
                    AF.Copy),
                    reads=["psbT"], writes=[f"xnf{ti}"])
            em.dma("sync", xn_sp[:, :, t0:t0 + TC], xn_fm[:],
                   reads=[f"xnf{ti}" for ti in range(4)],
                   writes=[f"xnsp{c}"], stream=f"xnsp{c % 2}")
            cp = c % 2
            xhf_keys = [f"xhf{cp}.{ti}.{k}" for ti in range(4)
                        for k in range(NK)]
            for jo in range(2 * NJ):
                bk = jo % 4
                def mmfn(h, jo=jo, bk=bk, cp=cp):
                    ins = None
                    for k in range(NK):
                        ins = h.matmul(
                            bankf(bk), win_sb[:, k, jo * 128:(jo + 1) * 128],
                            xh_fm[cp][:, k, :], start=(k == 0),
                            stop=(k == NK - 1))
                    return ins
                em.emit("tensor", mmfn, reads=["win"] + xhf_keys,
                        writes=[f"bank{bk}"])
                if jo < NJ:
                    em.emit("scalar", lambda h, jo=jo, bk=bk, cp=cp:
                            h.activation(upre[cp][:, jo, :], bankf(bk),
                                         AF.Copy),
                            reads=[f"bank{bk}"], writes=[f"up{cp}.{jo}"])
                else:
                    zp = jo % 2
                    em.emit("scalar", lambda h, bk=bk, zp=zp: h.activation(
                        szb[zp][:], bankf(bk), AF.Silu),
                        reads=[f"bank{bk}"], writes=[f"szb{zp}"])
                    em.dma("sync", sz_sp[:, jo - NJ, t0:t0 + TC],
                           szb[zp][:], reads=[f"szb{zp}"],
                           writes=[f"szsp{c}.{jo - NJ}"],
                           stream=f"szsp{zp}")
            em.dma("sync", up_sp[:, :, t0:t0 + TC], upre[cp][:],
                   reads=[f"up{cp}.{j}" for j in range(NJ)],
                   writes=[f"upsp{c}"], stream=f"upsp{cp}")
            drain_w0(5)

        em.fence()
        WA.reset()
        WK.reset()
        wdt_sb = WA.a([128, NJ, DI], BF16)
        upv = WK.a([128, NJ, TC + 3], BF16)
        uc2 = [WK.a([128, NJ, TC], BF16) for _ in range(2)]
        acc = [WK.a([128, TC], F32) for _ in range(2)]
        dpre = [WK.a([128, TC], F32) for _ in range(2)]
        dm = [WK.a([128, TC], F32) for _ in range(2)]
        dl = [WK.a([128, TC], F32) for _ in range(2)]
        dout = [WK.a([128, TC], F32) for _ in range(2)]
        wv = [WK.a([128, TC], BF16) for _ in range(2)]
        for k in range(NJ):
            em.dma("sync", wdt_sb[:, k, :], wf_dt[k * 128:(k + 1) * 128, :],
                   reads=["wfdt"], writes=["wdt"])
        upv_keys = [f"upv{j}" for j in range(NJ)]
        for c in range(NC):
            t0 = c * TC
            cp = c % 2
            uc = uc2[cp]
            if c == 0:
                em.emit("vector", lambda h: h.memset(upv[:, :, 0:3], 0.0),
                        writes=upv_keys)
                em.dma("sync", upv[:, :, 3:3 + TC], up_sp[:, :, 0:TC],
                       reads=["upsp0"], writes=upv_keys)
            else:
                em.dma("sync", upv[:],
                       up_sp[:, :, t0 - 3:t0 + TC],
                       reads=[f"upsp{c - 1}", f"upsp{c}"], writes=upv_keys)
            for j in range(NJ):
                a0, a1 = acc
                em.emit("vector", lambda h, j=j, a0=a0: h.tensor_scalar(
                    a0[:], upv[:, j, 0:TC], cwt3[:, 0, j:j + 1], None,
                    ALU.mult),
                    reads=[f"upv{j}", "cwt"], writes=["acc0"])
                em.emit("vector", lambda h, j=j, a0=a0, a1=a1:
                        h.scalar_tensor_tensor(
                            a1[:], upv[:, j, 1:1 + TC], cwt3[:, 1, j:j + 1],
                            a0[:], ALU.mult, ALU.add),
                        reads=[f"upv{j}", "cwt", "acc0"], writes=["acc1"])
                em.emit("vector", lambda h, j=j, a0=a0, a1=a1:
                        h.scalar_tensor_tensor(
                            a0[:], upv[:, j, 2:2 + TC], cwt3[:, 2, j:j + 1],
                            a1[:], ALU.mult, ALU.add),
                        reads=[f"upv{j}", "cwt", "acc1"], writes=["acc0"])
                em.emit("vector", lambda h, j=j, a0=a0, a1=a1:
                        h.scalar_tensor_tensor(
                            a1[:], upv[:, j, 3:3 + TC], cwt3[:, 3, j:j + 1],
                            a0[:], ALU.mult, ALU.add),
                        reads=[f"upv{j}", "cwt", "acc0"], writes=["acc1"])
                em.emit("scalar", lambda h, j=j, a1=a1, uc=uc: h.activation(
                    uc[:, j, :], a1[:], AF.Silu, bias=prmv(P_CB + j)),
                    reads=["acc1", "prm"], writes=[f"uc{cp}.{j}"])
            uc_keys = [f"uc{cp}.{j}" for j in range(NJ)]
            em.dma("sync", u_sp[:, :, t0:t0 + TC], uc[:],
                   reads=uc_keys, writes=[f"usp{c}"], stream=f"usp{cp}")
            def xpfn(h, uc=uc):
                ins = None
                for k in range(NJ):
                    ins = h.matmul(psf[0:16, 3072:3584], xp3[:, k, :],
                                   uc[:, k, :], start=(k == 0),
                                   stop=(k == NJ - 1))
                return ins
            em.emit("tensor", xpfn, reads=["xp16"] + uc_keys,
                    writes=["bank6"])
            em.emit("vector", lambda h, t0=t0: h.tensor_copy(
                bsb[0:16, t0:t0 + TC], psf[0:16, 3072:3584]),
                reads=["bank6"], writes=[f"bsb{c}"])
            for n in range(S):
                em.dma("sync", bmrep_sp[:, n, t0:t0 + TC],
                       bsb[n:n + 1, None, t0:t0 + TC].broadcast_to(
                           [1, 128, TC]),
                       reads=[f"bsb{c}"], writes=[f"bmr{c}.{n % 4}"],
                       stream=f"bmr{n % 4}")
            for jo in range(NJ):
                bk = jo % 4
                jp = jo % 2
                def mmfn(h, jo=jo, bk=bk, uc=uc):
                    ins = None
                    for k in range(NJ):
                        ins = h.matmul(
                            bankf(bk), wdt_sb[:, k, jo * 128:(jo + 1) * 128],
                            uc[:, k, :], start=(k == 0),
                            stop=(k == NJ - 1))
                    return ins
                em.emit("tensor", mmfn, reads=["wdt"] + uc_keys,
                        writes=[f"bank{bk}"])
                em.emit("vector", lambda h, bk=bk, jp=jp, jo=jo:
                        h.tensor_scalar(dpre[jp][:], bankf(bk),
                                        prmv(P_DTB + jo), None, ALU.add),
                        reads=[f"bank{bk}", "prm"], writes=[f"dpre{jp}"])
                em.emit("vector", lambda h, jp=jp: h.tensor_scalar(
                    dm[jp][:], dpre[jp][:], 43.0, None, ALU.min),
                    reads=[f"dpre{jp}"], writes=[f"dm{jp}"])
                em.emit("scalar", lambda h, jp=jp: h.activation(
                    dm[jp][:], dm[jp][:], AF.Exp),
                    reads=[f"dm{jp}"], writes=[f"dm{jp}"])
                em.emit("scalar", lambda h, jp=jp: h.activation(
                    dl[jp][:], dm[jp][:], AF.Ln, bias=cl(C_ONE)),
                    reads=[f"dm{jp}", f"c{C_ONE}"], writes=[f"dl{jp}"])
                em.emit("vector", lambda h, jp=jp: h.tensor_tensor(
                    dout[jp][:], dpre[jp][:], dl[jp][:], ALU.max),
                    reads=[f"dpre{jp}", f"dl{jp}"], writes=[f"dout{jp}"])
                em.dma("sync", d_sp[:, jo, t0:t0 + TC], dout[jp][:],
                       reads=[f"dout{jp}"], writes=[f"dsp{c}.{jo}"],
                       stream=f"dsp{jp}")
                em.emit("vector", lambda h, jp=jp, jo=jo, uc=uc:
                        h.tensor_tensor(wv[jp][:], uc[:, jo, :],
                                        dout[jp][:], ALU.mult),
                        reads=[f"uc{cp}.{jo}", f"dout{jp}"],
                        writes=[f"wv{jp}"])
                em.dma("sync", w_sp[:, jo, t0:t0 + TC], wv[jp][:],
                       reads=[f"wv{jp}"], writes=[f"wsp{c}.{jo}"],
                       stream=f"wsp{jp}")
            drain_w0(5)

        em.fence()
        WA.reset()
        WK.reset()
        WB.reset()
        brep = WA.a([128, S, TCB], BF16)
        hb = [WK.a([128, TCB], BF16) for _ in range(2)]
        dj = [WK.a([128, TCB], F32) for _ in range(2)]
        wj = [WK.a([128, TCB], BF16) for _ in range(2)]
        uj = [WK.a([128, TCB], BF16) for _ in range(2)]
        szj = [WK.a([128, TCB], BF16) for _ in range(2)]
        dab = [WK.a([128, TCB], F32) for _ in range(2)] + \
              [WB.a([128, TCB], F32) for _ in range(2)]
        pbf = [WK.a([128, TCB], BF16) for _ in range(4)]
        yb2 = [WK.a([128, 512], BF16) for _ in range(2)]
        NSP = TCB // 512

        for cq in range(NCB):
            q0 = cq * TCB
            p1c = tuple(range(cq * TCB // TC, (cq + 1) * TCB // TC))
            em.dma("sync", brep[:], bmrep_sp[:, :, q0:q0 + TCB],
                   reads=[f"bmr{cc}.{m}" for cc in p1c for m in range(4)],
                   writes=[f"brep{n}" for n in range(S)])
            for j in range(NJ):
                jp = j % 2
                em.dma("sync", dj[jp][:], d_sp[:, j, q0:q0 + TCB],
                       reads=[f"dsp{cc}.{j}" for cc in p1c],
                       writes=[f"dj{jp}"], stream=f"dj{jp}")
                em.dma("sync", wj[jp][:], w_sp[:, j, q0:q0 + TCB],
                       reads=[f"wsp{cc}.{j}" for cc in p1c],
                       writes=[f"wj{jp}"], stream=f"wj{jp}")
                em.dma("sync", uj[jp][:], u_sp[:, j, q0:q0 + TCB],
                       reads=[f"usp{cc}" for cc in p1c],
                       writes=[f"uj{jp}"], stream=f"uj{jp}")
                em.dma("sync", szj[jp][:], sz_sp[:, j, q0:q0 + TCB],
                       reads=[f"szsp{cc}.{j}" for cc in p1c],
                       writes=[f"szj{jp}"], stream=f"szj{jp}")
                for s in range(S):
                    sp = s % 4
                    hp = s % 2
                    em.emit("scalar", lambda h, j=j, s=s, sp=sp, jp=jp:
                            h.activation(dab[sp][:], dj[jp][:], AF.Exp,
                                         scale=aneg3[:, j, s:s + 1]),
                            reads=[f"dj{jp}", "aneg"], writes=[f"dab{sp}"])
                    em.emit("gpsimd", lambda h, s=s, sp=sp, jp=jp:
                            h.tensor_tensor(pbf[sp][:], wj[jp][:],
                                            brep[:, s, :], ALU.mult),
                            reads=[f"wj{jp}", f"brep{s}"],
                            writes=[f"pbf{sp}"])
                    init = 0.0 if cq == 0 else hst3[:, j, s:s + 1]
                    rd = [f"dab{sp}", f"pbf{sp}"]
                    if cq > 0:
                        rd.append(f"hst{j}.{s}")
                    em.emit("vector", lambda h, sp=sp, hp=hp, init=init:
                            h.tensor_tensor_scan(hb[hp][:], dab[sp][:],
                                                 pbf[sp][:], init, ALU.mult,
                                                 ALU.add),
                            reads=rd, writes=[f"hb{hp}"])
                    if cq < NCB - 1:
                        em.emit("vector", lambda h, j=j, s=s, hp=hp:
                                h.tensor_copy(hst3[:, j, s:s + 1],
                                              hb[hp][:, TCB - 1:TCB]),
                                reads=[f"hb{hp}"],
                                writes=[f"hst{j}.{s}"])
                    def yfn(h, s=s, hp=hp):
                        ins = None
                        for q in range(NSP):
                            ins = h.matmul(bankf(q), idb[:],
                                           hb[hp][:, q * 512:(q + 1) * 512],
                                           start=(s == 0), stop=False)
                        return ins
                    em.emit("tensor", yfn, reads=[f"hb{hp}", "idb"],
                            writes=[f"bank{q}" for q in range(NSP)])
                def udfn(h, jp=jp, j=j):
                    ins = None
                    for q in range(NSP):
                        ins = h.matmul(bankf(q), ddg3[:, j, :],
                                       uj[jp][:, q * 512:(q + 1) * 512],
                                       start=False, stop=True)
                    return ins
                em.emit("tensor", udfn, reads=[f"uj{jp}", "ddg"],
                        writes=[f"bank{q}" for q in range(NSP)])
                for q in range(NSP):
                    qp = q % 2
                    em.emit("vector", lambda h, q=q, qp=qp, jp=jp:
                            h.tensor_tensor(yb2[qp][:], bankf(q),
                                            szj[jp][:, q * 512:(q + 1) * 512],
                                            ALU.mult),
                            reads=[f"bank{q}", f"szj{jp}"],
                            writes=[f"yb2{qp}"])
                    em.dma("scalar",
                           yb_sp[:, j, q0 + q * 512:q0 + (q + 1) * 512],
                           yb2[qp][:], reads=[f"yb2{qp}"],
                           writes=[f"ybsp{NSP * cq + q}.{j}"],
                           stream=f"ybsp{qp}")

        em.fence()
        WA.reset()
        WK.reset()
        WB.reset()
        TC4, NC4 = 512, 8
        wfs = [WA.a([128, NK, D], BF16) for _ in range(3)]
        ybc = WK.a([128, NJ, TC4], BF16)
        xnc = WK.a([128, NK, TC4], BF16)
        wot = [WK.a([128, NJ, 128], BF16) for _ in range(2)]
        ssb = WK.a([128, NK, TC4], BF16)
        h1 = WK.a([128, NK, TC4], BF16)
        h2 = WK.a([128, NK, TC4], BF16)
        th = WK.a([128, NK, TC4], BF16)
        sg = WB.a([128, NK, TC4], F32)
        xres = [WK.a([128, D], F32) for _ in range(2)]
        outb = [WK.a([128, D], F32) for _ in range(2)]
        for i in range(3):
            for k in range(NK):
                em.dma("sync", wfs[i][:, k, :],
                       wf_f[i][k * 128:(k + 1) * 128, :],
                       reads=[f"wff{i}"], writes=[f"wf{i}"])
        for c in range(NC4):
            t0 = c * TC4
            em.dma("sync", ybc[:], yb_sp[:, :, t0:t0 + TC4],
                   reads=[f"ybsp{c}.{j}" for j in range(NJ)],
                   writes=["ybc"])
            em.dma("sync", xnc[:], xn_sp[:, :, t0:t0 + TC4],
                   reads=[f"xnsp{c}"], writes=["xnc"])
            for jo in range(NK):
                bk = jo % 2
                wp = jo % 2
                em.dma("sync", wot[wp][:],
                       wf_out[:, jo * 128:(jo + 1) * 128].rearrange(
                           "(k p) c -> p k c", p=128),
                       reads=["wfout"], writes=[f"wot{wp}"])
                def mofn(h, bk=bk, wp=wp):
                    ins = None
                    for k in range(NJ):
                        ins = h.matmul(bankf(bk, TC4), wot[wp][:, k, :],
                                       ybc[:, k, :],
                                       start=(k == 0), stop=(k == NJ - 1))
                    return ins
                em.emit("tensor", mofn, reads=[f"wot{wp}", "ybc"],
                        writes=[f"bank{bk}"])
                em.emit("vector", lambda h, jo=jo, bk=bk: h.tensor_tensor(
                    ssb[:, jo, :], bankf(bk, TC4), xnc[:, jo, :], ALU.add),
                    reads=[f"bank{bk}", "xnc"], writes=[f"ssb{jo}"])
            for st in range(3):
                wsb = wfs[st]
                src = (xnc, h1, h2)[st]
                srckeys = (["xnc"], [f"h1{j}" for j in range(NK)],
                           [f"h2{j}" for j in range(NK)])[st]
                for jo in range(NK):
                    bk = 2 + jo % 2
                    def mffn(h, st=st, jo=jo, bk=bk, wsb=wsb, src=src):
                        ins = None
                        for k in range(NK):
                            ins = h.matmul(
                                bankf(bk, TC4),
                                wsb[:, k, jo * 128:(jo + 1) * 128],
                                src[:, k, :], start=(k == 0),
                                stop=(k == NK - 1))
                        return ins
                    em.emit("tensor", mffn, reads=[f"wf{st}"] + srckeys,
                            writes=[f"bank{bk}"])
                    if st < 2:
                        dstt = (h1, h2)[st]
                        em.emit("scalar", lambda h, st=st, jo=jo, bk=bk,
                                dstt=dstt: h.activation(
                                    dstt[:, jo, :], bankf(bk, TC4), AF.Copy),
                                reads=[f"bank{bk}"],
                                writes=[f"h{st + 1}{jo}"])
                    else:
                        em.emit("scalar", lambda h, jo=jo, bk=bk:
                                h.activation(th[:, jo, :], bankf(bk, TC4),
                                             AF.Tanh, bias=prmv(P_FB + jo)),
                                reads=[f"bank{bk}", "prm"],
                                writes=[f"th{jo}"])
            em.emit("scalar", lambda h: h.activation(
                h1.rearrange("p k t -> p (k t)"),
                th.rearrange("p k t -> p (k t)"), AF.Silu),
                reads=[f"th{j}" for j in range(NK)],
                writes=[f"h1{j}" for j in range(NK)])
            em.emit("gpsimd", lambda h: h.tensor_tensor(
                sg.rearrange("p k t -> p (k t)"),
                ssb.rearrange("p k t -> p (k t)"),
                h1.rearrange("p k t -> p (k t)"), ALU.mult),
                reads=[f"ssb{j}" for j in range(NK)] +
                      [f"h1{j}" for j in range(NK)],
                writes=["sg"])
            for ti in range(4):
                par = ti % 2
                rows = slice(t0 + ti * 128, t0 + (ti + 1) * 128)
                def trfn(h, ti=ti):
                    ins = None
                    for dj in range(NK):
                        ins = h.transpose(
                            psf[:, 2048 + dj * 128:2048 + (dj + 1) * 128],
                            sg[:, dj, ti * 128:(ti + 1) * 128], idf[:])
                    return ins
                em.emit("tensor", trfn, reads=["sg", "idf"],
                        writes=["bank4", "bank5"])
                em.dma("sync", xres[par][:], x_tm[rows, :],
                       writes=[f"xres{par}"], stream=f"xres{par}")
                em.emit("vector", lambda h, par=par: h.tensor_tensor(
                    outb[par][:], psf[:, 2048:3072], xres[par][:], ALU.add),
                    reads=["bank4", "bank5", f"xres{par}"],
                    writes=[f"outb{par}"])
                em.dma("scalar", out_tm[rows, :], outb[par][:],
                       reads=[f"outb{par}"], writes=[f"out{c}.{ti}"],
                       stream=f"out{par}")

        em.emit("sync", lambda h: h.nop(),
                reads=[f"out{c}.{ti}" for c in range(8)
                       for ti in range(4)])

        smap = {}
        for sname in list(ESEM.values()) + sorted(em.streams):
            smap[sname] = e(nc.semaphore(sname))

        @blk.sync
        def _(h):
            em.run("sync", h, smap)

        @blk.vector
        def _(h):
            em.run("vector", h, smap)

        @blk.scalar
        def _(h):
            em.run("scalar", h, smap)

        @blk.tensor
        def _(h):
            em.run("tensor", h, smap)

        @blk.gpsimd
        def _(h):
            em.run("gpsimd", h, smap)
    assert w0_state["i"] == len(w0_jobs), \
        f"W0 jobs not drained: {w0_state['i']}/{len(w0_jobs)}"
    return nc


_CACHE = {}


def _get_program():
    if "nc" not in _CACHE:
        _CACHE["nc"] = build_program()
    return _CACHE["nc"]


def kernel(L0_attn_x, L0_attn_ln_gamma, L0_attn_ln_beta, L0_attn_in_proj_w,
           L0_attn_conv_w, L0_attn_conv_b, L0_attn_x_proj_w, L0_attn_dt_proj_w,
           L0_attn_dt_proj_b, L0_attn_A_log, L0_attn_D_param, L0_attn_out_proj_w,
           L0_ffn_weight, L0_ffn_aux_weight1, L0_ffn_aux_weight2, L0_ffn_bias):
    f32 = np.float32
    x = np.asarray(L0_attn_x, f32)
    B = x.shape[0]

    def vec_t(v, nt):
        return np.ascontiguousarray(np.asarray(v, f32).reshape(nt, 128).T)

    prm = np.zeros((128, 64), f32)
    prm[:, 0:16] = vec_t(L0_attn_conv_b, NJ)
    prm[:, 16:32] = vec_t(L0_attn_dt_proj_b, NJ)
    prm[:, 32:40] = vec_t(L0_attn_ln_gamma, NK)
    prm[:, 40:48] = vec_t(L0_attn_ln_beta, NK)
    prm[:, 48:56] = vec_t(L0_ffn_bias, NK)

    cw = np.asarray(L0_attn_conv_w, f32)
    cwt = np.ascontiguousarray(
        cw[:, 0, :].reshape(NJ, 128, 4).transpose(1, 2, 0).reshape(128, 64))

    dpar = np.asarray(L0_attn_D_param, f32)
    ddg = np.zeros((128, NJ, 128), f32)
    idx = np.arange(128)
    for j in range(NJ):
        ddg[idx, j, idx] = dpar[j * 128 + idx]
    ddg = np.ascontiguousarray(ddg.reshape(128, NJ * 128))

    xpw = np.asarray(L0_attn_x_proj_w, f32)[:, S:2 * S]
    xpw_t = np.ascontiguousarray(
        xpw.reshape(NJ, 128, S).transpose(1, 0, 2).reshape(128, NJ * S))

    alog_t = np.ascontiguousarray(
        np.asarray(L0_attn_A_log, f32).reshape(NJ, 128, S).transpose(1, 0, 2)
        .reshape(128, NJ * S))

    common = {
        "w_in": np.ascontiguousarray(np.asarray(L0_attn_in_proj_w, f32)),
        "w_dt": np.ascontiguousarray(np.asarray(L0_attn_dt_proj_w, f32)),
        "w_out": np.ascontiguousarray(np.asarray(L0_attn_out_proj_w, f32)),
        "w_f1": np.ascontiguousarray(np.asarray(L0_ffn_weight, f32)),
        "w_f2": np.ascontiguousarray(np.asarray(L0_ffn_aux_weight1, f32)),
        "w_f3": np.ascontiguousarray(np.asarray(L0_ffn_aux_weight2, f32)),
        "xpw_t": xpw_t,
        "cw_t": cwt,
        "ddg_t": ddg,
        "alog_t": alog_t,
        "prm_t": prm,
        "idb_t": np.eye(128, dtype=ml_dtypes.bfloat16),
        "idf_t": np.eye(128, dtype=f32),
    }
    in_maps = []
    for b in range(B):
        m = dict(common)
        m["x_tm"] = np.ascontiguousarray(x[b])
        in_maps.append(m)

    nc = _get_program()
    res = run_bass_kernel_spmd(nc, in_maps, core_ids=list(range(B)))
    return np.stack([np.asarray(res.results[b]["out_tm"]) for b in range(B)],
                    axis=0)


# version 17
# speedup vs baseline: 1.0076x; 1.0076x over previous
import re
import numpy as np
import ml_dtypes
import concourse.bass as bass
import concourse.mybir as mybir
from concourse.bass_utils import run_bass_kernel_spmd

F32 = mybir.dt.float32
BF16 = mybir.dt.bfloat16
AF = mybir.ActivationFunctionType
ALU = mybir.AluOpType

D, DI, S, T = 1024, 2048, 16, 4096
NJ, NK = DI // 128, D // 128
TC, NC = 512, 8
TCB, NCB = 2048, 2
EPS = 1e-5
ENGS = ("sync", "vector", "scalar", "tensor", "gpsimd")
ESEM = {"sync": "SY", "vector": "SV", "scalar": "SS", "tensor": "ST",
        "gpsimd": "SP"}


def _merge(dst, src):
    for k, v in src.items():
        if dst.get(k, -1) < v:
            dst[k] = v


class Em:
    def __init__(self):
        self.ops = {e: [] for e in ENGS}
        self.cnt = {}
        self.know = {e: {} for e in ENGS}
        self.snap = {}
        self.buf = {}
        self.pending_fence = None
        self.fenced = set()
        self.streams = set()

    def fence(self):
        self.pending_fence = dict(self.cnt)
        self.fenced = set()

    def emit(self, eng, fn, reads=(), writes=(), dma=False, stream=None):
        need = {}

        def add(sv):
            if need.get(sv[0], -1) < sv[1]:
                need[sv[0]] = sv[1]

        for k in reads:
            b = self.buf.get(k)
            if b and b["w"]:
                add(b["w"])
        for k in writes:
            b = self.buf.get(k)
            if b:
                if b["w"]:
                    add(b["w"])
                for sv in b["r"].items():
                    add(sv)
        if self.pending_fence is not None and eng not in self.fenced:
            for s, v in self.pending_fence.items():
                if v > 0:
                    add((s, v))
            self.fenced.add(eng)
            if len(self.fenced) == len(ENGS):
                self.pending_fence = None
        if dma:
            assert writes, "dma needs a write key to derive its stream"
            sem = "D_" + (stream or re.sub(r"[0-9.].*$", "", writes[0]))
            self.streams.add(sem)
            amt = 16
            prev = self.cnt.get(sem, 0)
            if prev > 0:
                add((sem, prev))
        else:
            sem = ESEM[eng]
            amt = 1
        know = self.know[eng]
        waits = [(s, v) for s, v in need.items() if know.get(s, -1) < v]
        for s, v in waits:
            _merge(know, self.snap.get((s, v), {}))
            if know.get(s, -1) < v:
                know[s] = v
        self.cnt[sem] = self.cnt.get(sem, 0) + amt
        my = self.cnt[sem]
        snap = dict(know)
        _merge(snap, self.snap.get((sem, my - amt), {}))
        snap[sem] = my
        self.snap[(sem, my)] = snap
        self.ops[eng].append((waits, fn, sem, amt))
        for k in writes:
            self.buf[k] = {"w": (sem, my), "r": {}}
        for k in reads:
            b = self.buf.setdefault(k, {"w": None, "r": {}})
            if b["r"].get(sem, -1) < my:
                b["r"][sem] = my
        return (sem, my)

    def dma(self, eng, out_ap, in_ap, reads=(), writes=(), stream=None):
        return self.emit(eng, lambda h: h.dma_start(out=out_ap, in_=in_ap),
                         reads=reads, writes=writes, dma=True, stream=stream)

    def run(self, eng, handle, smap):
        for waits, fn, sem, amt in self.ops[eng]:
            for s, v in waits:
                handle.wait_ge(smap[s], v)
            ins = fn(handle)
            if ins is not None:
                ins.then_inc(smap[sem], amt)


class Reg:

    def __init__(self, t, nbytes):
        self.t, self.cap, self.off = t, nbytes, 0

    def reset(self):
        self.off = 0

    def a(self, shape, dt):
        esz = 2 if dt == BF16 else 4
        n = int(np.prod(shape[1:])) * esz
        o = (self.off + 63) & ~63
        self.off = o + n
        assert self.off <= self.cap, f"region overflow {self.off}>{self.cap}"
        ap = self.t[:, o // 2:(o + n) // 2]
        if dt != BF16:
            ap = ap.bitcast(dt)
        if len(shape) > 2:
            free = shape[1:]
            letters = "abcdefg"[:len(free)]
            spec = "p (" + " ".join(letters) + ") -> p " + " ".join(letters)
            ap = ap.rearrange(spec, **{letters[i]: free[i]
                                       for i in range(len(free))})
        return ap


def build_program():
    nc = bass.Bass("TRN2", target_bir_lowering=False)

    def din(name, shape, dt=F32):
        return nc.dram_tensor(name, list(shape), dt, kind="ExternalInput").ap()

    x_tm = din("x_tm", [T, D])
    w_in = din("w_in", [D, 2 * DI])
    w_dt = din("w_dt", [DI, DI])
    w_out = din("w_out", [DI, D])
    w_f1 = din("w_f1", [D, D])
    w_f2 = din("w_f2", [D, D])
    w_f3 = din("w_f3", [D, D])
    xpw_t = din("xpw_t", [128, NJ * S])
    cw_t = din("cw_t", [128, 4 * NJ])
    ddg_t = din("ddg_t", [128, NJ * 128])
    alog_t = din("alog_t", [128, NJ * S])
    prm_t = din("prm_t", [128, 64])
    idb_t = din("idb_t", [128, 128], BF16)
    idf_t = din("idf_t", [128, 128], F32)
    out_tm = nc.dram_tensor("out_tm", [T, D], F32, kind="ExternalOutput").ap()

    def scr(name, shape, dt):
        return nc.dram_tensor(name, list(shape), dt, kind="Internal").ap()

    wf_dt = scr("wf_dt", [DI, DI], BF16)
    wf_out = scr("wf_out", [DI, D], BF16)
    wf_f = [scr(f"wf_f{i}", [D, D], BF16) for i in range(3)]
    xn_sp = scr("xn_sp", [128, NK, T], BF16)
    u_sp = scr("u_sp", [128, NJ, T], BF16)
    up_sp = scr("up_sp", [128, NJ, T], BF16)
    bmrep_sp = scr("bmrep_sp", [128, S, T], BF16)
    sz_sp = scr("sz_sp", [128, NJ, T], BF16)
    d_sp = scr("d_sp", [128, NJ, T], F32)
    w_sp = scr("w_sp", [128, NJ, T], BF16)
    yb_sp = scr("yb_sp", [128, NJ, T], BF16)

    em = Em()

    from contextlib import ExitStack
    with ExitStack() as stack:
        e = stack.enter_context
        wrega_t = e(nc.sbuf_tensor("wrega", [128, 32768], BF16))
        wregb_t = e(nc.sbuf_tensor("wregb", [128, 12288], BF16))
        work_t = e(nc.sbuf_tensor("work", [128, 43008], BF16))
        prm = e(nc.sbuf_tensor("prm", [128, 64], F32))
        cwt_s = e(nc.sbuf_tensor("cwt", [128, 64], F32))
        cells = e(nc.sbuf_tensor("cells", [128, 32], F32))
        aneg = e(nc.sbuf_tensor("aneg", [128, NJ * S], F32))
        ddg = e(nc.sbuf_tensor("ddg", [128, NJ * 128], BF16))
        xp16_s = e(nc.sbuf_tensor("xp16", [128, NJ * S], BF16))
        idb = e(nc.sbuf_tensor("idb", [128, 128], BF16))
        idf = e(nc.sbuf_tensor("idf", [128, 128], F32))
        bsb = e(nc.sbuf_tensor("bsb", [16, T], BF16))
        hstt = e(nc.sbuf_tensor("hstt", [128, NJ * S], F32))
        psf = e(nc.psum_tensor("psf", [128, 3584], F32))
        psb = e(nc.psum_tensor("psb", [128, 1024], BF16))
        blk = e(nc.Block())
        WA = Reg(wrega_t, 65536)
        WB = Reg(wregb_t, 24576)
        WK = Reg(work_t, 86016)
        aneg3 = aneg.rearrange("p (j s) -> p j s", j=NJ)
        hst3 = hstt.rearrange("p (j s) -> p j s", j=NJ)
        ddg3 = ddg.rearrange("p (j c) -> p j c", j=NJ)
        xp3 = xp16_s.rearrange("p (j s) -> p j s", j=NJ)
        cwt3 = cwt_s.rearrange("p (t j) -> p t j", t=4)
        P_CB, P_DTB, P_GAM, P_BET, P_FB = 0, 16, 32, 40, 48

        def prmv(i):
            return prm[:, i:i + 1]

        def cl(i):
            return cells[:, i:i + 1]

        def bankf(i, w=512):
            return psf[:, i * 512:i * 512 + w]

        wstg = [WB.a([128, 2048], F32) for _ in range(2)]
        wcv = [WB.a([128, 2048], BF16) for _ in range(2)]
        win_sb = WA.a([128, 8, 2 * DI], BF16)

        C_EPS, C_INVD, C_NEGH, C_ONE = 16, 17, 18, 19
        em.emit("vector", lambda h: h.memset(cl(C_EPS), EPS),
                writes=[f"c{C_EPS}"])
        em.emit("vector", lambda h: h.memset(cl(C_INVD), 1.0 / D),
                writes=[f"c{C_INVD}"])
        em.emit("vector", lambda h: h.memset(cl(C_NEGH), -0.5),
                writes=[f"c{C_NEGH}"])
        em.emit("vector", lambda h: h.memset(cl(C_ONE), 1.0),
                writes=[f"c{C_ONE}"])

        em.dma("sync", prm[:], prm_t, writes=["prm"])
        em.dma("sync", cwt_s[:], cw_t, writes=["cwt"])
        em.dma("sync", idb[:], idb_t, writes=["idb"])
        em.dma("sync", idf[:], idf_t, writes=["idf"])
        em.dma("sync", aneg[:], alog_t, writes=["aneg"])
        em.emit("scalar", lambda h: h.activation(aneg[:], aneg[:], AF.Exp),
                reads=["aneg"], writes=["aneg"])
        em.emit("vector", lambda h: h.tensor_scalar(
            aneg[:], aneg[:], -1.0, None, ALU.mult),
            reads=["aneg"], writes=["aneg"])
        em.dma("sync", wstg[0][:, 0:NJ * 128], ddg_t, writes=["wstg0"])
        em.emit("scalar", lambda h: h.activation(
            ddg[:], wstg[0][:, 0:NJ * 128], AF.Copy),
            reads=["wstg0"], writes=["ddg"])
        em.dma("sync", wstg[0][:, 0:NJ * S], xpw_t, writes=["wstg0"])
        em.emit("scalar", lambda h: h.activation(
            xp16_s[:], wstg[0][:, 0:NJ * S], AF.Copy),
            reads=["wstg0"], writes=["xp16"])

        w0_jobs = []
        for k in range(NK):
            for hh in range(2):
                w0_jobs.append(("win", k, hh, None))
        for k in range(NJ):
            w0_jobs.append(("dt", k, 0, wf_dt))
        for k in range(NJ):
            w0_jobs.append(("out", k, 0, wf_out))
        for i in range(3):
            for k in range(NK):
                w0_jobs.append((f"f{i}", k, 0, wf_f[i]))
        w0_state = {"i": 0}

        def drain_w0(njobs):
            for _ in range(njobs):
                i = w0_state["i"]
                if i >= len(w0_jobs):
                    return
                w0_state["i"] = i + 1
                kind, k, hh, dst = w0_jobs[i]
                par = i % 2
                if kind == "win":
                    src = w_in[k * 128:(k + 1) * 128, hh * 2048:(hh + 1) * 2048]
                    em.dma("sync", wstg[par][:], src,
                           reads=[], writes=[f"wstg{par}"],
                           stream=f"wstg{par}")
                    em.emit("scalar", lambda h, k=k, hh=hh, par=par:
                            h.activation(
                                win_sb[:, k, hh * 2048:(hh + 1) * 2048],
                                wstg[par][:], AF.Copy),
                            reads=[f"wstg{par}"], writes=["win"])
                else:
                    if kind == "dt":
                        src = w_dt[k * 128:(k + 1) * 128, :]
                        ncol = 2048
                        dkey = "wfdt"
                    elif kind == "out":
                        src = w_out[k * 128:(k + 1) * 128, :]
                        ncol = 1024
                        dkey = "wfout"
                    else:
                        src = (w_f1, w_f2, w_f3)[int(kind[1])][
                            k * 128:(k + 1) * 128, :]
                        ncol = 1024
                        dkey = "wf" + kind
                    em.dma("sync", wstg[par][:, 0:ncol], src,
                           writes=[f"wstg{par}"], stream=f"wstg{par}")
                    em.emit("scalar", lambda h, p=par, n=ncol: h.activation(
                        wcv[p][:, 0:n], wstg[p][:, 0:n], AF.Copy),
                        reads=[f"wstg{par}"], writes=[f"wcv{par}"])
                    em.dma("scalar", dst[k * 128:(k + 1) * 128, :],
                           wcv[par][:, 0:ncol],
                           reads=[f"wcv{par}"], writes=[dkey],
                           stream=f"wsc{par}")

        drain_w0(16)

        WK.reset()
        xb = [WK.a([128, D], F32) for _ in range(2)]
        dump = WK.a([128, D], F32)
        xn_tm = [WK.a([128, D], BF16) for _ in range(2)]
        xh_tm = [WK.a([128, D], BF16) for _ in range(2)]
        xh_fm = [WK.a([128, NK, TC], BF16) for _ in range(2)]
        xn_fm = WK.a([128, NK, TC], BF16)
        upre = [WK.a([128, NJ, TC], BF16) for _ in range(2)]
        szb = [WK.a([128, TC], BF16) for _ in range(2)]

        CSM, CSS, CMU, CVAR, CR, CNMR, CT6 = 0, 1, 2, 3, 4, 5, 6

        def ln_pass(src_ap, src_key, dst_ap, dst_key, base):
            sm, ss_, mu, var, r, nmr, t6 = (base + i for i in range(7))
            em.emit("vector", lambda h: h.tensor_tensor(
                dump[:], src_ap, src_ap, ALU.mult),
                reads=[src_key], writes=["dump"])
            em.emit("vector", lambda h: h.reduce_sum(
                cl(ss_), dump[:], mybir.AxisListType.X),
                reads=["dump"], writes=[f"c{ss_}"])
            em.emit("vector", lambda h: h.reduce_sum(
                cl(sm), src_ap, mybir.AxisListType.X),
                reads=[src_key], writes=[f"c{sm}"])
            em.emit("vector", lambda h: h.tensor_scalar(
                cl(mu), cl(sm), 1.0 / D, None, ALU.mult),
                reads=[f"c{sm}"], writes=[f"c{mu}"])
            em.emit("scalar", lambda h: h.activation(
                cl(t6), cl(mu), AF.Square),
                reads=[f"c{mu}"], writes=[f"c{t6}"])
            em.emit("vector", lambda h: h.scalar_tensor_tensor(
                cl(var), cl(ss_), 1.0 / D, cl(t6), ALU.mult, ALU.subtract),
                reads=[f"c{ss_}", f"c{t6}"], writes=[f"c{var}"])
            em.emit("scalar", lambda h: h.activation(
                cl(t6), cl(var), AF.Ln, bias=cl(C_EPS)),
                reads=[f"c{var}", f"c{C_EPS}"], writes=[f"c{t6}"])
            em.emit("scalar", lambda h: h.activation(
                cl(r), cl(t6), AF.Exp, scale=cl(C_NEGH)),
                reads=[f"c{t6}", f"c{C_NEGH}"], writes=[f"c{r}"])
            em.emit("vector", lambda h: h.scalar_tensor_tensor(
                cl(nmr), cl(mu), -1.0, cl(r), ALU.mult, ALU.mult),
                reads=[f"c{mu}", f"c{r}"], writes=[f"c{nmr}"])
            em.emit("vector", lambda h: h.tensor_scalar(
                dst_ap, src_ap, cl(r), cl(nmr), ALU.mult, ALU.add),
                reads=[src_key, f"c{r}", f"c{nmr}"], writes=[dst_key])

        for c in range(NC):
            t0 = c * TC
            for ti in range(4):
                par = ti % 2
                base = par * 8
                rows = slice(t0 + ti * 128, t0 + (ti + 1) * 128)
                em.dma("sync", xb[par][:], x_tm[rows, :],
                       writes=[f"xb{par}"], stream=f"xb{par}")
                ln_pass(xb[par][:], f"xb{par}", xn_tm[par][:], f"xn{par}",
                        base)
                ln_pass(xn_tm[par][:], f"xn{par}", xh_tm[par][:],
                        f"xh{par}", base)
                def trfn(h, p=par):
                    ins = None
                    for dj in range(NK):
                        ins = h.transpose(
                            psb[:, dj * 128:(dj + 1) * 128],
                            xh_tm[p][:, dj * 128:(dj + 1) * 128], idb[:])
                    return ins
                em.emit("tensor", trfn, reads=[f"xh{par}", "idb"],
                        writes=["psbT"])
                cp = c % 2
                for k in range(NK):
                    em.emit("vector", lambda h, k=k, ti=ti, cp=cp:
                            h.tensor_scalar(
                                xh_fm[cp][:, k, ti * 128:(ti + 1) * 128],
                                psb[:, k * 128:(k + 1) * 128],
                                prmv(P_GAM + k), prmv(P_BET + k),
                                ALU.mult, ALU.add),
                            reads=["psbT", "prm"],
                            writes=[f"xhf{cp}.{ti}.{k}"])
                def trfn2(h, p=par):
                    ins = None
                    for dj in range(NK):
                        ins = h.transpose(
                            psb[:, dj * 128:(dj + 1) * 128],
                            xn_tm[p][:, dj * 128:(dj + 1) * 128], idb[:])
                    return ins
                em.emit("tensor", trfn2, reads=[f"xn{par}", "idb"],
                        writes=["psbT"])
                em.emit("scalar", lambda h, ti=ti: h.activation(
                    xn_fm[:, :, ti * 128:(ti + 1) * 128],
                    psb.ap().rearrange("p (k t) -> p k t", k=NK),
                    AF.Copy),
                    reads=["psbT"], writes=[f"xnf{ti}"])
            em.dma("scalar", xn_sp[:, :, t0:t0 + TC], xn_fm[:],
                   reads=[f"xnf{ti}" for ti in range(4)],
                   writes=[f"xnsp{c}"], stream=f"xnsp{c % 2}")
            cp = c % 2
            xhf_keys = [f"xhf{cp}.{ti}.{k}" for ti in range(4)
                        for k in range(NK)]
            for jo in range(2 * NJ):
                bk = jo % 4
                def mmfn(h, jo=jo, bk=bk, cp=cp):
                    ins = None
                    for k in range(NK):
                        ins = h.matmul(
                            bankf(bk), win_sb[:, k, jo * 128:(jo + 1) * 128],
                            xh_fm[cp][:, k, :], start=(k == 0),
                            stop=(k == NK - 1))
                    return ins
                em.emit("tensor", mmfn, reads=["win"] + xhf_keys,
                        writes=[f"bank{bk}"])
                if jo < NJ:
                    em.emit("scalar", lambda h, jo=jo, bk=bk, cp=cp:
                            h.activation(upre[cp][:, jo, :], bankf(bk),
                                         AF.Copy),
                            reads=[f"bank{bk}"], writes=[f"up{cp}.{jo}"])
                else:
                    zp = jo % 2
                    em.emit("scalar", lambda h, bk=bk, zp=zp: h.activation(
                        szb[zp][:], bankf(bk), AF.Silu),
                        reads=[f"bank{bk}"], writes=[f"szb{zp}"])
                    em.dma("scalar", sz_sp[:, jo - NJ, t0:t0 + TC],
                           szb[zp][:], reads=[f"szb{zp}"],
                           writes=[f"szsp{c}.{jo - NJ}"],
                           stream=f"szsp{zp}")
            em.dma("scalar", up_sp[:, :, t0:t0 + TC], upre[cp][:],
                   reads=[f"up{cp}.{j}" for j in range(NJ)],
                   writes=[f"upsp{c}"], stream=f"upsp{cp}")
            drain_w0(5)

        em.fence()
        WA.reset()
        WK.reset()
        wdt_sb = WA.a([128, NJ, DI], BF16)
        upv = WK.a([128, NJ, TC + 3], BF16)
        uc2 = [WK.a([128, NJ, TC], BF16) for _ in range(2)]
        acc = [WK.a([128, TC], F32) for _ in range(2)]
        dpre = [WK.a([128, TC], F32) for _ in range(2)]
        dm = [WK.a([128, TC], F32) for _ in range(2)]
        dl = [WK.a([128, TC], F32) for _ in range(2)]
        dout = [WK.a([128, TC], F32) for _ in range(2)]
        wv = [WK.a([128, TC], BF16) for _ in range(2)]
        for k in range(NJ):
            em.dma("sync", wdt_sb[:, k, :], wf_dt[k * 128:(k + 1) * 128, :],
                   reads=["wfdt"], writes=["wdt"])
        upv_keys = [f"upv{j}" for j in range(NJ)]
        for c in range(NC):
            t0 = c * TC
            cp = c % 2
            uc = uc2[cp]
            if c == 0:
                em.emit("vector", lambda h: h.memset(upv[:, :, 0:3], 0.0),
                        writes=upv_keys)
                em.dma("sync", upv[:, :, 3:3 + TC], up_sp[:, :, 0:TC],
                       reads=["upsp0"], writes=upv_keys)
            else:
                em.dma("sync", upv[:],
                       up_sp[:, :, t0 - 3:t0 + TC],
                       reads=[f"upsp{c - 1}", f"upsp{c}"], writes=upv_keys)
            for j in range(NJ):
                a0, a1 = acc
                em.emit("vector", lambda h, j=j, a0=a0: h.tensor_scalar(
                    a0[:], upv[:, j, 0:TC], cwt3[:, 0, j:j + 1], None,
                    ALU.mult),
                    reads=[f"upv{j}", "cwt"], writes=["acc0"])
                em.emit("vector", lambda h, j=j, a0=a0, a1=a1:
                        h.scalar_tensor_tensor(
                            a1[:], upv[:, j, 1:1 + TC], cwt3[:, 1, j:j + 1],
                            a0[:], ALU.mult, ALU.add),
                        reads=[f"upv{j}", "cwt", "acc0"], writes=["acc1"])
                em.emit("vector", lambda h, j=j, a0=a0, a1=a1:
                        h.scalar_tensor_tensor(
                            a0[:], upv[:, j, 2:2 + TC], cwt3[:, 2, j:j + 1],
                            a1[:], ALU.mult, ALU.add),
                        reads=[f"upv{j}", "cwt", "acc1"], writes=["acc0"])
                em.emit("vector", lambda h, j=j, a0=a0, a1=a1:
                        h.scalar_tensor_tensor(
                            a1[:], upv[:, j, 3:3 + TC], cwt3[:, 3, j:j + 1],
                            a0[:], ALU.mult, ALU.add),
                        reads=[f"upv{j}", "cwt", "acc0"], writes=["acc1"])
                em.emit("scalar", lambda h, j=j, a1=a1, uc=uc: h.activation(
                    uc[:, j, :], a1[:], AF.Silu, bias=prmv(P_CB + j)),
                    reads=["acc1", "prm"], writes=[f"uc{cp}.{j}"])
            uc_keys = [f"uc{cp}.{j}" for j in range(NJ)]
            em.dma("scalar", u_sp[:, :, t0:t0 + TC], uc[:],
                   reads=uc_keys, writes=[f"usp{c}"], stream=f"usp{cp}")
            def xpfn(h, uc=uc):
                ins = None
                for k in range(NJ):
                    ins = h.matmul(psf[0:16, 3072:3584], xp3[:, k, :],
                                   uc[:, k, :], start=(k == 0),
                                   stop=(k == NJ - 1))
                return ins
            em.emit("tensor", xpfn, reads=["xp16"] + uc_keys,
                    writes=["bank6"])
            em.emit("vector", lambda h, t0=t0: h.tensor_copy(
                bsb[0:16, t0:t0 + TC], psf[0:16, 3072:3584]),
                reads=["bank6"], writes=[f"bsb{c}"])
            for n in range(S):
                em.dma("scalar", bmrep_sp[:, n, t0:t0 + TC],
                       bsb[n:n + 1, None, t0:t0 + TC].broadcast_to(
                           [1, 128, TC]),
                       reads=[f"bsb{c}"], writes=[f"bmr{c}.{n % 4}"],
                       stream=f"bmr{n % 4}")
            for jo in range(NJ):
                bk = jo % 4
                jp = jo % 2
                def mmfn(h, jo=jo, bk=bk, uc=uc):
                    ins = None
                    for k in range(NJ):
                        ins = h.matmul(
                            bankf(bk), wdt_sb[:, k, jo * 128:(jo + 1) * 128],
                            uc[:, k, :], start=(k == 0),
                            stop=(k == NJ - 1))
                    return ins
                em.emit("tensor", mmfn, reads=["wdt"] + uc_keys,
                        writes=[f"bank{bk}"])
                em.emit("vector", lambda h, bk=bk, jp=jp, jo=jo:
                        h.tensor_scalar(dpre[jp][:], bankf(bk),
                                        prmv(P_DTB + jo), None, ALU.add),
                        reads=[f"bank{bk}", "prm"], writes=[f"dpre{jp}"])
                em.emit("vector", lambda h, jp=jp: h.tensor_scalar(
                    dm[jp][:], dpre[jp][:], 43.0, None, ALU.min),
                    reads=[f"dpre{jp}"], writes=[f"dm{jp}"])
                em.emit("scalar", lambda h, jp=jp: h.activation(
                    dm[jp][:], dm[jp][:], AF.Exp),
                    reads=[f"dm{jp}"], writes=[f"dm{jp}"])
                em.emit("scalar", lambda h, jp=jp: h.activation(
                    dl[jp][:], dm[jp][:], AF.Ln, bias=cl(C_ONE)),
                    reads=[f"dm{jp}", f"c{C_ONE}"], writes=[f"dl{jp}"])
                em.emit("vector", lambda h, jp=jp: h.tensor_tensor(
                    dout[jp][:], dpre[jp][:], dl[jp][:], ALU.max),
                    reads=[f"dpre{jp}", f"dl{jp}"], writes=[f"dout{jp}"])
                em.dma("scalar", d_sp[:, jo, t0:t0 + TC], dout[jp][:],
                       reads=[f"dout{jp}"], writes=[f"dsp{c}.{jo}"],
                       stream=f"dsp{jp}")
                em.emit("vector", lambda h, jp=jp, jo=jo, uc=uc:
                        h.tensor_tensor(wv[jp][:], uc[:, jo, :],
                                        dout[jp][:], ALU.mult),
                        reads=[f"uc{cp}.{jo}", f"dout{jp}"],
                        writes=[f"wv{jp}"])
                em.dma("scalar", w_sp[:, jo, t0:t0 + TC], wv[jp][:],
                       reads=[f"wv{jp}"], writes=[f"wsp{c}.{jo}"],
                       stream=f"wsp{jp}")
            drain_w0(5)

        em.fence()
        WA.reset()
        WK.reset()
        WB.reset()
        brep = WA.a([128, S, TCB], BF16)
        hb = [WK.a([128, TCB], BF16) for _ in range(2)]
        dj = [WK.a([128, TCB], F32) for _ in range(2)]
        wj = [WK.a([128, TCB], BF16) for _ in range(2)]
        uj = [WK.a([128, TCB], BF16) for _ in range(2)]
        szj = [WK.a([128, TCB], BF16) for _ in range(2)]
        dab = [WK.a([128, TCB], F32) for _ in range(2)] + \
              [WB.a([128, TCB], F32) for _ in range(2)]
        pbf = [WK.a([128, TCB], BF16) for _ in range(4)]
        yb2 = [WK.a([128, 512], BF16) for _ in range(2)]
        NSP = TCB // 512

        for cq in range(NCB):
            q0 = cq * TCB
            p1c = tuple(range(cq * TCB // TC, (cq + 1) * TCB // TC))
            em.dma("sync", brep[:], bmrep_sp[:, :, q0:q0 + TCB],
                   reads=[f"bmr{cc}.{m}" for cc in p1c for m in range(4)],
                   writes=[f"brep{n}" for n in range(S)])
            for j in range(NJ):
                jp = j % 2
                em.dma("sync", dj[jp][:], d_sp[:, j, q0:q0 + TCB],
                       reads=[f"dsp{cc}.{j}" for cc in p1c],
                       writes=[f"dj{jp}"], stream=f"dj{jp}")
                em.dma("sync", wj[jp][:], w_sp[:, j, q0:q0 + TCB],
                       reads=[f"wsp{cc}.{j}" for cc in p1c],
                       writes=[f"wj{jp}"], stream=f"wj{jp}")
                em.dma("sync", uj[jp][:], u_sp[:, j, q0:q0 + TCB],
                       reads=[f"usp{cc}" for cc in p1c],
                       writes=[f"uj{jp}"], stream=f"uj{jp}")
                em.dma("sync", szj[jp][:], sz_sp[:, j, q0:q0 + TCB],
                       reads=[f"szsp{cc}.{j}" for cc in p1c],
                       writes=[f"szj{jp}"], stream=f"szj{jp}")
                for s in range(S):
                    sp = s % 4
                    hp = s % 2
                    em.emit("scalar", lambda h, j=j, s=s, sp=sp, jp=jp:
                            h.activation(dab[sp][:], dj[jp][:], AF.Exp,
                                         scale=aneg3[:, j, s:s + 1]),
                            reads=[f"dj{jp}", "aneg"], writes=[f"dab{sp}"])
                    em.emit("gpsimd", lambda h, s=s, sp=sp, jp=jp:
                            h.tensor_tensor(pbf[sp][:], wj[jp][:],
                                            brep[:, s, :], ALU.mult),
                            reads=[f"wj{jp}", f"brep{s}"],
                            writes=[f"pbf{sp}"])
                    init = 0.0 if cq == 0 else hst3[:, j, s:s + 1]
                    rd = [f"dab{sp}", f"pbf{sp}"]
                    if cq > 0:
                        rd.append(f"hst{j}.{s}")
                    em.emit("vector", lambda h, sp=sp, hp=hp, init=init:
                            h.tensor_tensor_scan(hb[hp][:], dab[sp][:],
                                                 pbf[sp][:], init, ALU.mult,
                                                 ALU.add),
                            reads=rd, writes=[f"hb{hp}"])
                    if cq < NCB - 1:
                        em.emit("vector", lambda h, j=j, s=s, hp=hp:
                                h.tensor_copy(hst3[:, j, s:s + 1],
                                              hb[hp][:, TCB - 1:TCB]),
                                reads=[f"hb{hp}"],
                                writes=[f"hst{j}.{s}"])
                    def yfn(h, s=s, hp=hp):
                        ins = None
                        for q in range(NSP):
                            ins = h.matmul(bankf(q), idb[:],
                                           hb[hp][:, q * 512:(q + 1) * 512],
                                           start=(s == 0), stop=False)
                        return ins
                    em.emit("tensor", yfn, reads=[f"hb{hp}", "idb"],
                            writes=[f"bank{q}" for q in range(NSP)])
                def udfn(h, jp=jp, j=j):
                    ins = None
                    for q in range(NSP):
                        ins = h.matmul(bankf(q), ddg3[:, j, :],
                                       uj[jp][:, q * 512:(q + 1) * 512],
                                       start=False, stop=True)
                    return ins
                em.emit("tensor", udfn, reads=[f"uj{jp}", "ddg"],
                        writes=[f"bank{q}" for q in range(NSP)])
                for q in range(NSP):
                    qp = q % 2
                    em.emit("vector", lambda h, q=q, qp=qp, jp=jp:
                            h.tensor_tensor(yb2[qp][:], bankf(q),
                                            szj[jp][:, q * 512:(q + 1) * 512],
                                            ALU.mult),
                            reads=[f"bank{q}", f"szj{jp}"],
                            writes=[f"yb2{qp}"])
                    em.dma("scalar",
                           yb_sp[:, j, q0 + q * 512:q0 + (q + 1) * 512],
                           yb2[qp][:], reads=[f"yb2{qp}"],
                           writes=[f"ybsp{NSP * cq + q}.{j}"],
                           stream=f"ybsp{qp}")

        em.fence()
        WA.reset()
        WK.reset()
        WB.reset()
        TC4, NC4 = 512, 8
        wfs = [WA.a([128, NK, D], BF16) for _ in range(3)]
        ybc = WK.a([128, NJ, TC4], BF16)
        xnc = WK.a([128, NK, TC4], BF16)
        wot = [WK.a([128, NJ, 128], BF16) for _ in range(2)]
        ssb = WK.a([128, NK, TC4], BF16)
        h1 = WK.a([128, NK, TC4], BF16)
        h2 = WK.a([128, NK, TC4], BF16)
        th = WK.a([128, NK, TC4], BF16)
        sg = WB.a([128, NK, TC4], F32)
        xres = [WK.a([128, D], F32) for _ in range(2)]
        outb = [WK.a([128, D], F32) for _ in range(2)]
        for i in range(3):
            for k in range(NK):
                em.dma("sync", wfs[i][:, k, :],
                       wf_f[i][k * 128:(k + 1) * 128, :],
                       reads=[f"wff{i}"], writes=[f"wf{i}"])
        for c in range(NC4):
            t0 = c * TC4
            em.dma("sync", ybc[:], yb_sp[:, :, t0:t0 + TC4],
                   reads=[f"ybsp{c}.{j}" for j in range(NJ)],
                   writes=["ybc"])
            em.dma("sync", xnc[:], xn_sp[:, :, t0:t0 + TC4],
                   reads=[f"xnsp{c}"], writes=["xnc"])
            for jo in range(NK):
                bk = jo % 2
                wp = jo % 2
                em.dma("sync", wot[wp][:],
                       wf_out[:, jo * 128:(jo + 1) * 128].rearrange(
                           "(k p) c -> p k c", p=128),
                       reads=["wfout"], writes=[f"wot{wp}"])
                def mofn(h, bk=bk, wp=wp):
                    ins = None
                    for k in range(NJ):
                        ins = h.matmul(bankf(bk, TC4), wot[wp][:, k, :],
                                       ybc[:, k, :],
                                       start=(k == 0), stop=(k == NJ - 1))
                    return ins
                em.emit("tensor", mofn, reads=[f"wot{wp}", "ybc"],
                        writes=[f"bank{bk}"])
                em.emit("vector", lambda h, jo=jo, bk=bk: h.tensor_tensor(
                    ssb[:, jo, :], bankf(bk, TC4), xnc[:, jo, :], ALU.add),
                    reads=[f"bank{bk}", "xnc"], writes=[f"ssb{jo}"])
            for st in range(3):
                wsb = wfs[st]
                src = (xnc, h1, h2)[st]
                srckeys = (["xnc"], [f"h1{j}" for j in range(NK)],
                           [f"h2{j}" for j in range(NK)])[st]
                for jo in range(NK):
                    bk = 2 + jo % 2
                    def mffn(h, st=st, jo=jo, bk=bk, wsb=wsb, src=src):
                        ins = None
                        for k in range(NK):
                            ins = h.matmul(
                                bankf(bk, TC4),
                                wsb[:, k, jo * 128:(jo + 1) * 128],
                                src[:, k, :], start=(k == 0),
                                stop=(k == NK - 1))
                        return ins
                    em.emit("tensor", mffn, reads=[f"wf{st}"] + srckeys,
                            writes=[f"bank{bk}"])
                    if st < 2:
                        dstt = (h1, h2)[st]
                        em.emit("scalar", lambda h, st=st, jo=jo, bk=bk,
                                dstt=dstt: h.activation(
                                    dstt[:, jo, :], bankf(bk, TC4), AF.Copy),
                                reads=[f"bank{bk}"],
                                writes=[f"h{st + 1}{jo}"])
                    else:
                        em.emit("scalar", lambda h, jo=jo, bk=bk:
                                h.activation(th[:, jo, :], bankf(bk, TC4),
                                             AF.Tanh, bias=prmv(P_FB + jo)),
                                reads=[f"bank{bk}", "prm"],
                                writes=[f"th{jo}"])
            em.emit("scalar", lambda h: h.activation(
                h1.rearrange("p k t -> p (k t)"),
                th.rearrange("p k t -> p (k t)"), AF.Silu),
                reads=[f"th{j}" for j in range(NK)],
                writes=[f"h1{j}" for j in range(NK)])
            em.emit("gpsimd", lambda h: h.tensor_tensor(
                sg.rearrange("p k t -> p (k t)"),
                ssb.rearrange("p k t -> p (k t)"),
                h1.rearrange("p k t -> p (k t)"), ALU.mult),
                reads=[f"ssb{j}" for j in range(NK)] +
                      [f"h1{j}" for j in range(NK)],
                writes=["sg"])
            for ti in range(4):
                par = ti % 2
                rows = slice(t0 + ti * 128, t0 + (ti + 1) * 128)
                def trfn(h, ti=ti):
                    ins = None
                    for dj in range(NK):
                        ins = h.transpose(
                            psf[:, 2048 + dj * 128:2048 + (dj + 1) * 128],
                            sg[:, dj, ti * 128:(ti + 1) * 128], idf[:])
                    return ins
                em.emit("tensor", trfn, reads=["sg", "idf"],
                        writes=["bank4", "bank5"])
                em.dma("sync", xres[par][:], x_tm[rows, :],
                       writes=[f"xres{par}"], stream=f"xres{par}")
                em.emit("vector", lambda h, par=par: h.tensor_tensor(
                    outb[par][:], psf[:, 2048:3072], xres[par][:], ALU.add),
                    reads=["bank4", "bank5", f"xres{par}"],
                    writes=[f"outb{par}"])
                em.dma("scalar", out_tm[rows, :], outb[par][:],
                       reads=[f"outb{par}"], writes=[f"out{c}.{ti}"],
                       stream=f"out{par}")

        em.emit("sync", lambda h: h.nop(),
                reads=[f"out{c}.{ti}" for c in range(8)
                       for ti in range(4)])

        smap = {}
        for sname in list(ESEM.values()) + sorted(em.streams):
            smap[sname] = e(nc.semaphore(sname))

        @blk.sync
        def _(h):
            em.run("sync", h, smap)

        @blk.vector
        def _(h):
            em.run("vector", h, smap)

        @blk.scalar
        def _(h):
            em.run("scalar", h, smap)

        @blk.tensor
        def _(h):
            em.run("tensor", h, smap)

        @blk.gpsimd
        def _(h):
            em.run("gpsimd", h, smap)
    assert w0_state["i"] == len(w0_jobs), \
        f"W0 jobs not drained: {w0_state['i']}/{len(w0_jobs)}"
    return nc


_CACHE = {}


def _get_program():
    if "nc" not in _CACHE:
        _CACHE["nc"] = build_program()
    return _CACHE["nc"]


def kernel(L0_attn_x, L0_attn_ln_gamma, L0_attn_ln_beta, L0_attn_in_proj_w,
           L0_attn_conv_w, L0_attn_conv_b, L0_attn_x_proj_w, L0_attn_dt_proj_w,
           L0_attn_dt_proj_b, L0_attn_A_log, L0_attn_D_param, L0_attn_out_proj_w,
           L0_ffn_weight, L0_ffn_aux_weight1, L0_ffn_aux_weight2, L0_ffn_bias):
    f32 = np.float32
    x = np.asarray(L0_attn_x, f32)
    B = x.shape[0]

    def vec_t(v, nt):
        return np.ascontiguousarray(np.asarray(v, f32).reshape(nt, 128).T)

    prm = np.zeros((128, 64), f32)
    prm[:, 0:16] = vec_t(L0_attn_conv_b, NJ)
    prm[:, 16:32] = vec_t(L0_attn_dt_proj_b, NJ)
    prm[:, 32:40] = vec_t(L0_attn_ln_gamma, NK)
    prm[:, 40:48] = vec_t(L0_attn_ln_beta, NK)
    prm[:, 48:56] = vec_t(L0_ffn_bias, NK)

    cw = np.asarray(L0_attn_conv_w, f32)
    cwt = np.ascontiguousarray(
        cw[:, 0, :].reshape(NJ, 128, 4).transpose(1, 2, 0).reshape(128, 64))

    dpar = np.asarray(L0_attn_D_param, f32)
    ddg = np.zeros((128, NJ, 128), f32)
    idx = np.arange(128)
    for j in range(NJ):
        ddg[idx, j, idx] = dpar[j * 128 + idx]
    ddg = np.ascontiguousarray(ddg.reshape(128, NJ * 128))

    xpw = np.asarray(L0_attn_x_proj_w, f32)[:, S:2 * S]
    xpw_t = np.ascontiguousarray(
        xpw.reshape(NJ, 128, S).transpose(1, 0, 2).reshape(128, NJ * S))

    alog_t = np.ascontiguousarray(
        np.asarray(L0_attn_A_log, f32).reshape(NJ, 128, S).transpose(1, 0, 2)
        .reshape(128, NJ * S))

    common = {
        "w_in": np.ascontiguousarray(np.asarray(L0_attn_in_proj_w, f32)),
        "w_dt": np.ascontiguousarray(np.asarray(L0_attn_dt_proj_w, f32)),
        "w_out": np.ascontiguousarray(np.asarray(L0_attn_out_proj_w, f32)),
        "w_f1": np.ascontiguousarray(np.asarray(L0_ffn_weight, f32)),
        "w_f2": np.ascontiguousarray(np.asarray(L0_ffn_aux_weight1, f32)),
        "w_f3": np.ascontiguousarray(np.asarray(L0_ffn_aux_weight2, f32)),
        "xpw_t": xpw_t,
        "cw_t": cwt,
        "ddg_t": ddg,
        "alog_t": alog_t,
        "prm_t": prm,
        "idb_t": np.eye(128, dtype=ml_dtypes.bfloat16),
        "idf_t": np.eye(128, dtype=f32),
    }
    in_maps = []
    for b in range(B):
        m = dict(common)
        m["x_tm"] = np.ascontiguousarray(x[b])
        in_maps.append(m)

    nc = _get_program()
    res = run_bass_kernel_spmd(nc, in_maps, core_ids=list(range(B)))
    return np.stack([np.asarray(res.results[b]["out_tm"]) for b in range(B)],
                    axis=0)
